# version 11
# baseline (speedup 1.0000x reference)
import numpy as np
import ml_dtypes
from contextlib import ExitStack

import concourse.bass as bass
import concourse.mybir as mybir
from concourse.bass_utils import run_bass_kernel_spmd

F32 = mybir.dt.float32
BF16 = mybir.dt.bfloat16
AF = mybir.ActivationFunctionType
ALU = mybir.AluOpType
AX = mybir.AxisListType

D = 1024
SEG = 1024
NOWN = 2048
NH = 3072
CSHIFT = -45.0

_TAPS = {}


class Ph:

    def __init__(self, k):
        self.k = k
        self.snap = dict(k.v)
        self.ops = {"sync": [], "gpsimd": [], "tensor": [], "vector": [], "scalar": []}

    def _add(self, eng, fn):
        self.ops[eng].append(fn)

    def op(self, eng, fn, inc=False):
        k = self.k
        if inc:
            sem = {"tensor": "sT", "vector": "sV", "scalar": "sS", "gpsimd": "sG2"}[eng]
            v = k.bump(sem)
            s = k.sems[sem]
            self._add(eng, lambda E: fn(E).then_inc(s, 1))
            return v
        self._add(eng, fn)
        return None

    def dma(self, eng, out, in_):
        sem = "sL" if eng in ("sync", "scalar") else "sG"
        v = self.k.bump(sem, 16)
        s = self.k.sems[sem]
        self._add(eng, lambda E: E.dma_start(out=out, in_=in_).then_inc(s, 16))
        return v

    def wait(self, eng, sem, val=None):
        v = self.k.v[sem] if val is None else val
        if v <= 0:
            return
        s = self.k.sems[sem]
        self._add(eng, lambda E: E.wait_ge(s, v))

    def emit(self):
        nc = self.k.nc
        active = [(e, fns) for e, fns in self.ops.items() if fns]
        if not active:
            return
        pre = {}
        for eng in ("sync", "gpsimd"):
            w = []
            for s in ("sT", "sV", "sS"):
                v = self.snap[s]
                if v > 0:
                    w.append((self.k.sems[s], v))
            pre[eng] = w
        for eng in ("tensor", "vector", "scalar"):
            w = []
            for s in ("sL", "sG"):
                v = self.snap[s]
                if v > 0:
                    w.append((self.k.sems[s], v))
            pre[eng] = w
        with nc.Block() as blk:
            for eng, fns in active:
                def cb(E, fns=fns, w=pre[eng]):
                    for sem, val in w:
                        E.wait_ge(sem, val)
                    for f in fns:
                        f(E)
                getattr(blk, eng)(cb)


class K:
    def __init__(self, nc, sems):
        self.nc = nc
        self.sems = sems
        self.v = {name: 0 for name in sems}

    def bump(self, name, n=1):
        self.v[name] += n
        return self.v[name]


def build_program(upto=99, taps=()):
    nc = bass.Bass("TRN2", target_bir_lowering=False, debug=False)

    di = lambda n, s, d=F32: nc.dram_tensor(n, s, d, kind="ExternalInput")
    x_in = di("x_in", [NH, D])
    wq_d = di("wq", [D, D]); wkv_d = di("wkv", [D, D])
    wo_d = [di("wo_a", [D, D]), di("wo_b", [D, D])]
    w0t_d = di("w0t", [D, D])
    wz_d = [di("wzf", [D, 512]), di("wzs", [D, 512])]
    wh_d = [di("whf", [D, 512]), di("whs", [D, 512])]
    wg_d = di("wg", [D, D])
    w1t_d = di("w1t", [D, 4096])
    w2t_d = di("w2t", [4096, D])
    bo_d = [di("bo_a", [128, 8]), di("bo_b", [128, 8])]
    bz_d = [di("bzf", [128, 4]), di("bzs", [128, 4])]
    bh_d = [di("bhf", [128, 4]), di("bhs", [128, 4])]
    bg_d = di("bg", [128, 8]); b1_d = di("b1", [128, 32]); b2_d = di("b2", [128, 8])
    cosk_d = di("cos_k", [NH, 64], BF16); sink_d = di("sinm_k", [NH, 64], BF16)
    cosq_d = di("cos_q", [NOWN, 64], BF16); sinq_d = di("sinm_q", [NOWN, 64], BF16)
    iden_d = di("iden", [128, 128])
    causal_d = di("causal", [128, 128])
    flags_d = di("flags", [128, 2])
    selm_d = di("selmask", [128, 64])
    sel31_d = di("sel31", [128, 64])

    y_d = nc.dram_tensor("y", [NOWN, D], F32, kind="ExternalOutput")
    x1s = nc.dram_tensor("x1s", [NOWN, D], F32)
    g0s = nc.dram_tensor("g0s", [8, 128, NOWN], BF16)
    s1s = nc.dram_tensor("s1s", [8, 128, NOWN], BF16)
    wq2s = nc.dram_tensor("wq2s", [4096, D], BF16)
    agin = nc.dram_tensor("agin", [128, 8], F32)
    thrb = nc.dram_tensor("thrb", [1, 4], F32)
    agout = nc.dram_tensor("agout", [8, 128, 8], F32)

    ctx = ExitStack()
    sb = lambda n, s, d: ctx.enter_context(nc.sbuf_tensor(n, s, d))

    A1 = sb("A1", [128, 16384], BF16)
    A2 = sb("A2", [128, 8192], BF16)
    A3 = sb("A3", [128, 16384], BF16)
    A4 = sb("A4", [128, 16512], BF16)
    A6 = sb("A6", [128, 4096], BF16)
    SCB = sb("SCB", [128, 4096], BF16)
    UA = sb("UA", [128, 2048], BF16)
    WT = sb("WT", [128, 8192], BF16)
    FA = sb("FA", [128, 6144], F32)
    COSK = sb("COSK", [128, 1536], BF16); SINK = sb("SINK", [128, 1536], BF16)
    COSQ = sb("COSQ", [128, 1024], BF16); SINQ = sb("SINQ", [128, 1024], BF16)
    IDENF = sb("IDENF", [128, 128], F32)
    CAUS = sb("CAUS", [128, 128], BF16)
    FLAGS = sb("FLAGS", [128, 2], F32)
    SELM = sb("SELM", [128, 64], F32)
    ONES = sb("ONES", [1, 64], BF16)
    SEL31 = sb("SEL31", [128, 64], BF16)
    BO = [sb("BOA", [128, 8], F32), sb("BOB", [128, 8], F32)]
    BZ = [sb("BZF", [128, 4], F32), sb("BZS", [128, 4], F32)]
    BZN = [sb("BZFN", [128, 4], F32), sb("BZSN", [128, 4], F32)]
    BH = [sb("BHF", [128, 4], F32), sb("BHS", [128, 4], F32)]
    BG = sb("BG", [128, 8], F32); B1 = sb("B1", [128, 32], F32); B2 = sb("B2", [128, 8], F32)
    ST = sb("ST", [128, 16], F32)
    STATE = sb("STATE", [128, 8], F32)
    STATE2 = sb("STATE2", [128, 8], F32)
    CARRY = sb("CARRY", [128, 8], F32)
    GATH = sb("GATH", [128, 64], F32)
    MSKD = sb("MSKD", [128, 64], F32)
    THR = sb("THR", [128, 4], F32)
    GAC = sb("GAC", [128, 2], F32)
    GTMP = sb("GTMP", [1, 2], F32)
    RCPB = sb("RCPB", [1, 1024], BF16)
    CONSTP = sb("CONSTP", [128, 12], F32)

    _CONSTS = [1e-5, CSHIFT, -1.0, 0.5 / (1024.0 * 4096.0), 0.5e-5, 0.0, 1.0,
               -0.5 / (1024.0 * 4096.0), -0.5e-5]
    for _i, _v in enumerate(_CONSTS):
        nc.const_aps.register(F32, _v, CONSTP[:, _i:_i + 1])

    sems = {}
    for name in ("sL", "sG", "sT", "sV", "sS", "sG2", "cc"):
        sems[name] = ctx.enter_context(nc.semaphore(name))
    k = K(nc, sems)

    tap_specs = []

    def tap(name, ap, shape, dtype=F32):
        if name not in taps:
            return
        t = nc.dram_tensor("tap_" + name, shape, dtype, kind="ExternalOutput")
        _TAPS[name] = "tap_" + name
        tap_specs.append((t, ap))

    def tapd(name, dram, shape, dtype=F32):
        if name not in taps:
            return
        t = nc.dram_tensor("tap_" + name, shape, dtype, kind="ExternalOutput")
        _TAPS[name] = "tap_" + name
        tap_specs.append((t, dram.ap()))

    def flush_taps():
        if not tap_specs:
            return
        ph = Ph(k)
        for t, ap in list(tap_specs):
            ph.dma("sync", t.ap(), ap)
        ph.wait("sync", "sL")
        ph.emit()
        tap_specs.clear()

    def fa(a, b):
        return FA[:, a:b]

    psum_ctx = ExitStack()
    PF = psum_ctx.enter_context(nc.psum_tensor("PF", [128, 3072], F32))
    PB = psum_ctx.enter_context(nc.psum_tensor("PB", [128, 2048], BF16))

    ph = Ph(k)
    ph.dma("sync", COSK[:, :], bass.AP(cosk_d, 0, [[64, 128], [8192, 24], [1, 64]]))
    ph.dma("sync", SINK[:, :], bass.AP(sink_d, 0, [[64, 128], [8192, 24], [1, 64]]))
    ph.dma("sync", COSQ[:, :], bass.AP(cosq_d, 0, [[64, 128], [8192, 16], [1, 64]]))
    ph.dma("sync", SINQ[:, :], bass.AP(sinq_d, 0, [[64, 128], [8192, 16], [1, 64]]))
    ph.dma("sync", FLAGS[:, :], flags_d[:, :])
    ph.dma("sync", SELM[:, :], selm_d[:, :])
    ph.dma("sync", IDENF[:, :], iden_d[:, :])
    for i in range(2):
        ph.dma("sync", BO[i][:, :], bo_d[i][:, :])
        ph.dma("sync", BZ[i][:, :], bz_d[i][:, :])
        ph.dma("sync", BH[i][:, :], bh_d[i][:, :])
    ph.dma("sync", BG[:, :], bg_d[:, :])
    ph.dma("sync", B1[:, :], b1_d[:, :])
    ph.dma("sync", B2[:, :], b2_d[:, :])
    ph.dma("gpsimd", CAUS[:, :], causal_d[:, :])
    ph.dma("gpsimd", SEL31[:, :], sel31_d[:, :])
    for _i, _v in enumerate(_CONSTS):
        ph.op("gpsimd", lambda E, i=_i, v=_v: E.memset(CONSTP[:, i:i + 1], v))
    ph.op("gpsimd", lambda E: E.memset(ONES[0:1, :], 1.0))
    for kt in range(16):
        ph.op("gpsimd", lambda E, kt=kt: E.memset(
            bass.AP(A4, 1032 * kt + 64, [[16512, 128], [129, 8]]), 1.0))
    ph.op("gpsimd", lambda E: E.memset(STATE[:, :], 0.0))
    ph.op("gpsimd", lambda E: E.memset(GAC[:, :], 0.0))
    ph.wait("vector", "sL")
    for i in range(2):
        ph.op("vector", lambda E, i=i: E.tensor_scalar_mul(BZN[i][:, :], BZ[i][:, :], -1.0))
    ph.emit()

    def ln_tiles(tiles, src_row_of, dstT, dst_ncols):
        ph = Ph(k)
        ph.wait("tensor", "sG", k.v["sG"])
        slot_consumer = {}
        for pos, i in enumerate(tiles):
            sl = pos % 2
            xt = fa(1024 * sl, 1024 * (sl + 1))
            if sl in slot_consumer:
                ph.wait("sync", "sS", slot_consumer[sl])
            lw = ph.dma("sync", xt, src_row_of(i))
            ph.wait("vector", "sL", lw)
            ph.wait("vector", "sS", k.v["sS"])
            rr = ph.op("vector", lambda E, xt=xt: E.reduce_sum(
                ST[:, 0:1], xt, axis=AX.X), inc=True)
            ph.wait("scalar", "sL", lw)
            sq = ph.op("scalar", lambda E, xt=xt: E.activation(
                SCB[:, 2048:3072], xt, AF.Square, accum_out=ST[:, 1:2]), inc=True)
            ph.wait("vector", "sS", sq)
            ph.wait("vector", "sV", rr)
            va = ph.op("vector", lambda E: E.tensor_scalar_mul(
                ST[:, 2:3], ST[:, 0:1], 1.0 / 1024.0), inc=True)
            ph.op("vector", lambda E: E.tensor_scalar_mul(ST[:, 3:4], ST[:, 1:2], 1.0 / 1024.0))
            ph.wait("vector", "sV", va)
            vb = ph.op("vector", lambda E: E.tensor_mul(
                ST[:, 4:5], ST[:, 2:3], ST[:, 2:3]), inc=True)
            ph.wait("vector", "sV", vb)
            v1 = ph.op("vector", lambda E: E.tensor_sub(ST[:, 5:6], ST[:, 3:4], ST[:, 4:5]),
                       inc=True)
            ph.wait("scalar", "sV", v1)
            s1_ = ph.op("scalar", lambda E: E.activation(
                ST[:, 6:7], ST[:, 5:6], AF.Sqrt, bias=1e-5), inc=True)
            ph.wait("vector", "sS", s1_)
            vc = ph.op("vector", lambda E: E.reciprocal(ST[:, 7:8], ST[:, 6:7]), inc=True)
            ph.wait("vector", "sV", vc)
            v2 = ph.op("vector", lambda E: E.scalar_tensor_tensor(
                ST[:, 8:9], ST[:, 2:3], -1.0, ST[:, 7:8],
                op0=ALU.mult, op1=ALU.mult), inc=True)
            xnb = SCB[:, 1024 * sl:1024 * (sl + 1)]
            ph.wait("scalar", "sV", v2)
            ph.wait("scalar", "sT", k.v["sT"])
            s2_ = ph.op("scalar", lambda E, xt=xt, xnb=xnb: E.activation(
                xnb, xt, AF.Identity, scale=ST[:, 7:8], bias=ST[:, 8:9]), inc=True)
            slot_consumer[sl] = s2_
            pb = PB[:, 1024 * sl:1024 * (sl + 1)]
            ph.wait("tensor", "sS", s2_)
            ph.wait("tensor", "sV", k.v["sV"])
            for c in range(8):
                ph.op("tensor", lambda E, xnb=xnb, c=c, pb=pb: E.transpose(
                    pb[:, 128 * c:128 * c + 128], xnb[:, 128 * c:128 * c + 128],
                    _IDENB[0][:, :]), inc=(c == 7))
            ph.wait("vector", "sT", k.v["sT"])
            for c in range(8):
                ph.op("vector", lambda E, pb=pb, c=c, off=128 * pos, dt=dstT,
                      dc=dst_ncols: E.tensor_copy(
                          bass.AP(dt, dc * c + off, [[dt.shape[1], 128], [1, 128]]),
                          pb[:, 128 * c:128 * c + 128]), inc=(c == 7))
        ph.emit()

    _IDENB = [None]

    def make_idenb():
        ph = Ph(k)
        idb = sb("IDENB", [128, 128], BF16)
        ph.dma("gpsimd", idb[:, :], iden_d[:, :])
        ph.emit()
        _IDENB[0] = idb

    make_idenb()

    if upto >= 1:
        ln_tiles(list(range(8, 24)), lambda i: x_in[128 * i:128 * i + 128, :], A1, 2048)
        ln_tiles(list(range(0, 8)), lambda i: x_in[128 * i:128 * i + 128, :], A2, 1024)
        tap("xnT", A1[:, :], [128, 16384], BF16)
        tap("xnH", A2[:, :], [128, 8192], BF16)
        flush_taps()

    def proj_fm(ph, lhs_of, rhsT, rhs_ncols, rhs_off, m_list, nb_list, ps_of,
                evac, kchunks=8):
        for m in m_list:
            for nb, nbo, nbw in nb_list:
                ps = ps_of(m, nb, nbw)
                for kc in range(kchunks):
                    rhs = bass.AP(rhsT, rhs_ncols * kc + rhs_off + nbo,
                                  [[rhsT.shape[1], 128], [1, nbw]])
                    inc = kc == kchunks - 1
                    ph.op("tensor", lambda E, ps=ps, l=lhs_of(kc, m), r=rhs,
                          st=(kc == 0), sp=inc: E.matmul(ps, l, r, start=st, stop=sp),
                          inc=inc)
                evac(m, nb, ps)

    if upto >= 1.2:
        ph = Ph(k)
        sc_ = {}
        for kc in range(8):
            sl = kc % 2
            wst = fa(4096 + 1024 * sl, 4096 + 1024 * (sl + 1))
            if sl in sc_:
                ph.wait("sync", "sS", sc_[sl])
            lw = ph.dma("sync", wst, w0t_d[128 * kc:128 * kc + 128, :])
            ph.wait("scalar", "sL", lw)
            sc_[sl] = ph.op("scalar", lambda E, wst=wst, kc=kc: E.activation(
                WT[:, 1024 * kc:1024 * kc + 1024], wst, AF.Sign), inc=True)
        ph.emit()

    if upto >= 1.4:
        ph = Ph(k)
        ph.wait("tensor", "sS", k.v["sS"])

        g0_slot_dma = {}

        def evac_g0(m, nb, ps):
            ph.wait("scalar", "sT", k.v["sT"])
            slot = nb % 4
            dst = SCB[:, 512 * slot:512 * slot + 512]
            if slot in g0_slot_dma:
                ph.wait("scalar", "sL", g0_slot_dma[slot])
            sv = ph.op("scalar", lambda E, ps=ps, dst=dst: E.activation(
                dst, ps, AF.Silu), inc=True)
            ph.wait("sync", "sS", sv)
            g0_slot_dma[slot] = ph.dma(
                "sync", bass.AP(g0s, NOWN * 128 * m + 512 * nb,
                                [[NOWN, 128], [1, 512]]), dst)

        cnt = [0]

        def ps_g0(m, nb, nbw):
            r = PF[:, 512 * (cnt[0] % 2):512 * (cnt[0] % 2) + 512]
            cnt[0] += 1
            ph.wait("tensor", "sS", max(0, k.v["sS"] - 1))
            return r

        proj_fm(ph, lambda kc, m: WT[:, 1024 * kc + 128 * m:1024 * kc + 128 * m + 128],
                A1, 2048, 0, range(8), [(nb, 512 * nb, 512) for nb in range(4)],
                ps_g0, evac_g0)
        ph.emit()
        tap("g0s", None, None) if False else None

    if upto >= 1.6:
        ph = Ph(k)
        gsl = {}
        for col, src, rows, ncols in ((0, w1t_d, 1024, 4096), (1, w2t_d, 4096, 1024)):
            for kc in range(rows // 128):
                for h in range(ncols // 1024):
                    sl = (kc * (ncols // 1024) + h) % 2
                    wst = fa(4096 + 1024 * sl, 4096 + 1024 * (sl + 1))
                    if sl in gsl:
                        ph.wait("sync", "sS", gsl[sl])
                    lw = ph.dma("sync", wst, bass.AP(
                        src, ncols * 128 * kc + 1024 * h, [[ncols, 128], [1, 1024]]))
                    ph.wait("scalar", "sL", lw)
                    ph.wait("scalar", "sV", k.v["sV"])
                    sv = ph.op("scalar", lambda E, wst=wst: E.activation(
                        SCB[:, 0:1024], wst, AF.Abs, accum_out=ST[:, 0:1]), inc=True)
                    gsl[sl] = sv
                    ph.wait("vector", "sS", sv)
                    ga_ = ph.op("vector", lambda E, col=col: E.tensor_add(
                        GAC[:, col:col + 1], GAC[:, col:col + 1], ST[:, 0:1]), inc=True)
                    ph.wait("vector", "sV", ga_)
        ph.wait("tensor", "sV", k.v["sV"])
        ph.op("tensor", lambda E: E.transpose(
            PF[0:1, 0:128], GAC[:, 0:1], IDENF[:, :]))
        tv = ph.op("tensor", lambda E: E.transpose(
            PF[0:1, 512:640], GAC[:, 1:2], IDENF[:, :]), inc=True)
        ph.wait("vector", "sT", tv)
        ph.op("vector", lambda E: E.reduce_sum(GTMP[0:1, 0:1], PF[0:1, 0:128],
                                               axis=AX.X))
        ph.op("vector", lambda E: E.reduce_sum(GTMP[0:1, 1:2], PF[0:1, 512:640],
                                               axis=AX.X), inc=True)
        ph.emit()

    if upto >= 1.8:
        ph = Ph(k)
        ph.wait("scalar", "sV", k.v["sV"])
        for idx, npar in ((0, 1024 * 4096), (1, 4096 * 1024)):
            ph.op("scalar", lambda E, idx=idx, npar=npar: E.activation(
                THR[0:1, 2 * idx:2 * idx + 1], GTMP[0:1, idx:idx + 1], AF.Identity,
                scale=0.5 / npar, bias=0.5e-5))
            ph.op("scalar", lambda E, idx=idx, npar=npar: E.activation(
                THR[0:1, 2 * idx + 1:2 * idx + 2], GTMP[0:1, idx:idx + 1],
                AF.Identity, scale=-0.5 / npar, bias=-0.5e-5), inc=True)
        ph.wait("gpsimd", "sS", k.v["sS"])
        ph.dma("gpsimd", thrb[0:1, :], THR[0:1, 0:4])
        ph.wait("gpsimd", "sG", k.v["sG"])
        ph.dma("gpsimd", THR[:, 0:4], bass.AP(thrb, 0, [[0, 128], [1, 4]]))
        ph.emit()

    if upto >= 2:
        ph = Ph(k)
        ph.wait("vector", "sG", k.v["sG"])
        qsl = {}
        for kc in range(32):
            sl = kc % 2
            wst = fa(4096 + 1024 * sl, 4096 + 1024 * (sl + 1))
            if sl in qsl:
                ph.wait("sync", "sV", qsl[sl])
            lw = ph.dma("sync", wst, w2t_d[128 * kc:128 * kc + 128, :])
            t1b = SCB[:, 0:1024]; t2b = SCB[:, 1024:2048]; qb_ = SCB[:, 2048:3072]
            ph.wait("vector", "sL", k.v["sL"])
            ph.op("vector", lambda E, wst=wst, t1b=t1b: E.tensor_scalar(
                t1b, wst, THR[:, 2:3], None, op0=ALU.is_ge))
            qv2 = ph.op("vector", lambda E, wst=wst, t2b=t2b: E.tensor_scalar(
                t2b, wst, THR[:, 3:4], None, op0=ALU.is_le), inc=True)
            ph.wait("vector", "sV", qv2)
            vv = ph.op("vector", lambda E, t1b=t1b, t2b=t2b, qb_=qb_: E.tensor_sub(
                qb_, t1b, t2b), inc=True)
            qsl[sl] = vv
            ph.wait("sync", "sV", vv)
            ph.dma("sync", wq2s[128 * kc:128 * kc + 128, :], qb_)
        ph.emit()

    def gen_kv(tiles, slot_of):
        ph = Ph(k)
        ph.dma("gpsimd", WT[:, :],
               bass.AP(wkv_d, 0, [[1024, 128], [131072, 8], [1, 1024]]))
        ph.wait("tensor", "sG", k.v["sG"])
        rot_done = {}
        for pos_, i in enumerate(tiles):
            halo = i < 8
            rT, rn, ro = (A2, 1024, 128 * i) if halo else (A1, 2048, 128 * (i - 8))
            pk = PF[:, 0:1024]
            if 0 in rot_done:
                ph.wait("tensor", "sV", rot_done[0])
            for nb in range(2):
                for kc in range(8):
                    lhs = bass.AP(rT, rn * kc + ro, [[rT.shape[1], 128], [1, 128]])
                    rhs = WT[:, 1024 * kc + 512 * nb:1024 * kc + 512 * nb + 512]
                    ph.op("tensor", lambda E, pk=pk, nb=nb, l=lhs, r=rhs,
                          st=(kc == 0), sp=(kc == 7): E.matmul(
                              pk[:, 512 * nb:512 * nb + 512], l, r, start=st, stop=sp),
                          inc=(kc == 7))
            ph.wait("vector", "sT", k.v["sT"])
            rot_done[0] = _rotary(ph, COSK, SINK, 1536, 64 * i, pk)
            slot, wcol = slot_of(i)
            kvp = SCB[:, 0:1024]
            v1 = ph.op("vector", lambda E, slot=slot: E.tensor_copy(
                bass.AP(A4, 1032 * slot, [[16512, 128], [129, 8], [1, 64]]),
                bass.AP(SCB, 0, [[4096, 128], [128, 8], [1, 64]])))
            v2 = ph.op("vector", lambda E, slot=slot: E.tensor_copy(
                bass.AP(A4, 1032 * slot + 65, [[16512, 128], [129, 8], [1, 64]]),
                bass.AP(SCB, 64, [[4096, 128], [128, 8], [1, 64]])), inc=True)
            pb = PB[:, 1024 * (i % 2):1024 * (i % 2) + 1024]
            ph.wait("tensor", "sV", k.v["sV"])
            for c in range(8):
                ph.op("tensor", lambda E, c=c, pb=pb, kvp=kvp: E.transpose(
                    pb[:, 128 * c:128 * c + 128], kvp[:, 128 * c:128 * c + 128],
                    _IDENB[0][:, :]), inc=(c == 7))
            ph.wait("vector", "sT", k.v["sT"])
            for c in range(8):
                ph.op("vector", lambda E, pb=pb, c=c, wcol=wcol: E.tensor_copy(
                    bass.AP(A3, 2048 * c + wcol, [[16384, 128], [1, 128]]),
                    pb[:, 128 * c:128 * c + 128]), inc=(c == 7))
        ph.emit()

    def _rotary(ph, ctab, stab, tabn, tco, pk):
        tmp = SCB[:, 1024:2048]
        tmp2 = SCB[:, 2048:3072]
        stg = FA[:, 0:1024]
        pkfree = ph.op("vector", lambda E, pk=pk: E.tensor_copy(stg, pk), inc=True)
        ph.op("vector", lambda E: E.tensor_mul(
            bass.AP(SCB, 1024, [[4096, 128], [64, 16], [1, 32]]),
            bass.AP(FA, 32, [[6144, 128], [64, 16], [1, 32]]),
            bass.AP(stab, tco, [[tabn, 128], [0, 16], [1, 32]])))
        ph.op("vector", lambda E: E.tensor_mul(
            bass.AP(SCB, 1024 + 32, [[4096, 128], [64, 16], [1, 32]]),
            bass.AP(FA, 0, [[6144, 128], [64, 16], [1, 32]]),
            bass.AP(stab, tco + 32, [[tabn, 128], [0, 16], [1, 32]])))
        rv_ = ph.op("vector", lambda E: E.tensor_mul(
            bass.AP(SCB, 2048, [[4096, 128], [64, 16], [1, 64]]),
            bass.AP(FA, 0, [[6144, 128], [64, 16], [1, 64]]),
            bass.AP(ctab, tco, [[tabn, 128], [0, 16], [1, 64]])), inc=True)
        ph.wait("vector", "sV", rv_)
        ph.op("vector", lambda E: E.tensor_add(SCB[:, 0:1024], tmp, tmp2), inc=True)
        return pkfree

    def gen_q(tiles):
        ph = Ph(k)
        ph.dma("gpsimd", WT[:, :],
               bass.AP(wq_d, 0, [[1024, 128], [131072, 8], [1, 1024]]))
        ph.wait("tensor", "sG", k.v["sG"])
        rot_done = {}
        for pos, i in enumerate(tiles):
            pk = PF[:, 0:1024]
            if 0 in rot_done:
                ph.wait("tensor", "sV", rot_done[0])
            for nb in range(2):
                for kc in range(8):
                    lhs = bass.AP(A1, 2048 * kc + 128 * i, [[16384, 128], [1, 128]])
                    rhs = WT[:, 1024 * kc + 512 * nb:1024 * kc + 512 * nb + 512]
                    ph.op("tensor", lambda E, pk=pk, nb=nb, l=lhs, r=rhs,
                          st=(kc == 0), sp=(kc == 7): E.matmul(
                              pk[:, 512 * nb:512 * nb + 512], l, r, start=st, stop=sp),
                          inc=(kc == 7))
            ph.wait("vector", "sT", k.v["sT"])
            rot_done[0] = _rotary(ph, COSQ, SINQ, 1024, 64 * i, pk)
            pb = PB[:, 1024 * (pos % 2):1024 * (pos % 2) + 1024]
            ph.wait("tensor", "sV", k.v["sV"])
            for c in range(8):
                ph.op("tensor", lambda E, c=c, pb=pb: E.transpose(
                    pb[:, 128 * c:128 * c + 128], SCB[:, 128 * c:128 * c + 128],
                    _IDENB[0][:, :]), inc=(c == 7))
            ph.wait("vector", "sT", k.v["sT"])
            for c in range(8):
                ph.op("vector", lambda E, pb=pb, c=c, off=128 * pos: E.tensor_copy(
                    bass.AP(A2, 1024 * c + off, [[8192, 128], [1, 128]]),
                    pb[:, 128 * c:128 * c + 128]), inc=(c == 7))
        ph.emit()

    def attention_seg(seg, b1_slots, b1_wcol, b2_slots, b2_wcol):
        ph = Ph(k)
        ph.dma("gpsimd", WT[:, :],
               bass.AP(wo_d[seg], 0, [[1024, 128], [131072, 8], [1, 1024]]))
        ph.emit()

        for qb in range(2):
            ph = Ph(k)
            ph.wait("tensor", "sG", k.v["sG"])
            for hg in range(8):
                psS = [PF[:, 0:512], PF[:, 512:1024], PF[:, 1024:1536], PF[:, 1536:2048]]
                psOA = PF[:, 2048:2560]
                psOB = PF[:, 2560:3072]
                kts = [(s_, b1_wcol + 128 * j, 0, True, False)
                       for j, s_ in enumerate(b1_slots)]
                for j, s_ in enumerate(b2_slots):
                    c0r = 128 * j - 512 * qb
                    if c0r >= 512:
                        continue
                    kts.append((s_, b2_wcol + 128 * j, max(c0r, 0), False, c0r >= 0))
                expsem = [0, 0]
                pvsem = [0, 0]
                pend = []

                def emit_scores(idx, slot, wcol, c0, isb1, diag):
                    w = 512 - c0
                    sl = idx % 2
                    uA = UA[:, 1024 * sl:1024 * sl + w]
                    uB = UA[:, 1024 * sl + 512:1024 * sl + 512 + w]
                    pA = psS[2 * sl][:, 0:w]
                    pB = psS[2 * sl + 1][:, 0:w]
                    if expsem[sl]:
                        ph.wait("tensor", "sS", expsem[sl])
                    qo = 1024 * hg + 512 * qb + c0
                    kvo = 2048 * hg + wcol
                    ph.op("tensor", lambda E, pA=pA, kvo=kvo, qo=qo, w=w:
                          E.matmul(pA, A3[0:64, kvo:kvo + 128],
                                   A2[0:64, qo:qo + w], start=True, stop=True,
                                   tile_position=(0, 0)))
                    smm = ph.op("tensor", lambda E, pB=pB, kvo=kvo, qo=qo, w=w:
                                E.matmul(pB, A3[64:128, kvo:kvo + 128],
                                         A2[64:128, qo:qo + w], start=True, stop=True,
                                         tile_position=(64, 0)), inc=True)
                    ph.wait("scalar", "sT", smm)
                    if pvsem[sl]:
                        ph.wait("scalar", "sT", pvsem[sl])
                    bias = FLAGS[:, seg:seg + 1] if isb1 else CSHIFT
                    ph.op("scalar", lambda E, uA=uA, pA=pA, bias=bias:
                          E.activation(uA, pA, AF.Exp, bias=bias))
                    se = ph.op("scalar", lambda E, uB=uB, pB=pB, bias=bias:
                               E.activation(uB, pB, AF.Exp, bias=bias), inc=True)
                    expsem[sl] = se
                    dep = ("sS", se)
                    if diag:
                        ph.wait("vector", "sS", se)
                        ph.op("vector", lambda E, uA=uA: E.tensor_mul(
                            uA[:, 0:128], uA[:, 0:128], CAUS[:, :]))
                        dv = ph.op("vector", lambda E, uB=uB: E.tensor_mul(
                            uB[:, 0:128], uB[:, 0:128], CAUS[:, :]), inc=True)
                        dep = ("sV", dv)
                    return (uA, uB, slot, c0, w, dep, sl)

                def emit_pv(item, st, hg=None):
                    uA, uB, slot, c0, w, dep, sl = item
                    ph.wait("tensor", dep[0], dep[1])
                    ph.op("tensor", lambda E, slot=slot, uA=uA, c0=c0, w=w, st=st,
                          hg=hg: E.matmul(
                              psOA[:, c0:c0 + w],
                              A4[:, 1032 * slot + 129 * hg:1032 * slot + 129 * hg + 128],
                              uA, start=st, stop=False))
                    pv = ph.op("tensor", lambda E, slot=slot, uB=uB, c0=c0, w=w,
                               st=st, hg=hg: E.matmul(
                                   psOB[:, c0:c0 + w],
                                   A4[:, 1032 * slot + 129 * hg + 1:
                                      1032 * slot + 129 * hg + 129],
                                   uB, start=st, stop=False), inc=True)
                    pvsem[sl] = pv

                first = True
                for idx, (slot, wcol, c0, isb1, diag) in enumerate(kts):
                    pend.append(emit_scores(idx, slot, wcol, c0, isb1, diag))
                    if len(pend) > 1:
                        emit_pv(pend.pop(0), first, hg=hg)
                        first = False
                while pend:
                    emit_pv(pend.pop(0), first, hg=hg)
                    first = False
                ph.wait("vector", "sT", k.v["sT"])

                def _rcpA(E, psOA=psOA):
                    with nc.allow_low_precision(reason="bf16 matmul operand"):
                        return E.reciprocal(RCPB[0:1, 0:512], psOA[64:65, :])

                def _rcpB(E, psOB=psOB):
                    with nc.allow_low_precision(reason="bf16 matmul operand"):
                        return E.reciprocal(SCB[32:64, 0:512], psOB[32:64, :])

                ph.op("vector", _rcpA)
                rv = ph.op("vector", _rcpB, inc=True)
                ph.wait("tensor", "sV", rv)
                pbc = PF[:, 0:512]
                ph.op("tensor", lambda E, pbc=pbc: E.matmul(
                    pbc[0:64, :], ONES[0:1, 0:64], RCPB[0:1, 0:512],
                    start=True, stop=True, tile_position=(0, 0)))
                bt = ph.op("tensor", lambda E, pbc=pbc: E.matmul(
                    pbc[64:128, :], SEL31[32:64, :], SCB[32:64, 0:512],
                    start=True, stop=True, tile_position=(32, 64)), inc=True)
                ph.wait("vector", "sT", bt)
                rb = ph.op("vector", lambda E, pbc=pbc: E.tensor_copy(
                    SCB[:, 512:1024], pbc[:, :]), inc=True)
                ph.wait("vector", "sV", rb)
                ph.op("vector", lambda E, psOA=psOA, hg=hg: E.tensor_mul(
                    A6[0:64, 512 * hg:512 * hg + 512], psOA[0:64, :],
                    SCB[0:64, 512:1024]))
                ev = ph.op("vector", lambda E, psOB=psOB, hg=hg: E.tensor_mul(
                    A6[64:128, 512 * hg:512 * hg + 512], psOB[64:128, :],
                    SCB[64:128, 512:1024]), inc=True)
                ph.wait("tensor", "sV", ev)
            ph.emit()

            ph = Ph(k)
            sgbuf = []
            sg_sem = {}
            for m in range(8):
                psP = PF[:, 512 * (m % 2):512 * (m % 2) + 512]
                g0c = SCB[:, 512 * (m % 2):512 * (m % 2) + 512]
                if (m % 2) in sg_sem:
                    ph.wait("sync", "sV", sg_sem[m % 2])
                lw = ph.dma("sync", g0c, bass.AP(
                    g0s, NOWN * 128 * m + 1024 * seg + 512 * qb, [[NOWN, 128], [1, 512]]))
                ph.wait("tensor", "sV", max(0, k.v["sV"] - 1))
                for kc in range(8):
                    ph.op("tensor", lambda E, psP=psP, kc=kc, m=m, st=(kc == 0),
                          sp=(kc == 7): E.matmul(
                              psP, WT[:, 1024 * kc + 128 * m:1024 * kc + 128 * m + 128],
                              bass.AP(A6, 512 * kc, [[4096, 128], [1, 512]]),
                              start=st, stop=sp), inc=(kc == 7))
                if m < 6:
                    sg = SCB[:, 1024 + 512 * m:1024 + 512 * m + 512]
                else:
                    sg = UA[:, 512 * (m - 6):512 * (m - 6) + 512]
                sgbuf.append(sg)
                ph.wait("vector", "sT", k.v["sT"])
                ph.wait("vector", "sL", lw)
                sg_sem[m % 2] = ph.op(
                    "vector", lambda E, psP=psP, sg=sg, m=m, g0c=g0c, seg=seg:
                    E.scalar_tensor_tensor(sg, psP, BO[seg][:, m:m + 1], g0c,
                                           op0=ALU.add, op1=ALU.mult), inc=True)
            for tt in range(4):
                pb = PB[:, 1024 * (tt % 2):1024 * (tt % 2) + 1024]
                ph.wait("tensor", "sV", k.v["sV"])
                for m in range(8):
                    ph.op("tensor", lambda E, pb=pb, m=m, sg=sgbuf[m], tt=tt:
                          E.transpose(pb[:, 128 * m:128 * m + 128],
                                      sg[:, 128 * tt:128 * tt + 128], _IDENB[0][:, :]),
                          inc=(m == 7))
                row = 1024 * seg + 512 * qb + 128 * tt
                xt = fa(0, 1024)
                xo = fa(1024, 2048)
                lw = ph.dma("sync", xt, x_in[1024 + row:1024 + row + 128, :])
                ph.wait("vector", "sT", k.v["sT"])
                ph.wait("vector", "sL", lw)
                ph.wait("vector", "sL", k.v["sL"])
                cv = ph.op("vector", lambda E, pb=pb, xt=xt, xo=xo: E.tensor_add(
                    xo, pb, xt), inc=True)
                ph.wait("sync", "sV", cv)
                ph.dma("sync", x1s[row:row + 128, :], xo)
                ph.wait("vector", "sL", k.v["sL"])
            ph.emit()

    if upto == 2:
        tapd("g0s", g0s, [8, 128, NOWN], BF16)
        tapd("wq2s", wq2s, [4096, D], BF16)
        tap("thr", THR[:, :], [128, 4])
        flush_taps()
    if upto >= 3:
        gen_kv(list(range(0, 16)), lambda i: (i, 128 * i))
        gen_q(list(range(0, 8)))
        tap("kvT", A3[:, :], [128, 16384], BF16)
        tap("kvTok", A4[:, :], [128, 16512], BF16)
        tap("qT", A2[:, :], [128, 8192], BF16)
        flush_taps()
    if upto >= 4:
        attention_seg(0, list(range(0, 8)), 0, list(range(8, 16)), 1024)
        tap("aoT_a", A6[:, :], [128, 4096], BF16)
        flush_taps()
    if upto >= 5:
        gen_kv(list(range(16, 24)), lambda i: (i - 16, 128 * (i - 16)))
        gen_q(list(range(8, 16)))
    if upto >= 6:
        attention_seg(1, list(range(8, 16)), 1024, list(range(0, 8)), 0)
        tapd("x1s", x1s, [NOWN, D])
        flush_taps()

    if upto >= 7:
        ln_tiles(list(range(16)), lambda i: x1s[128 * i:128 * i + 128, :], A1, 2048)
        tap("xn1T", A1[:, :], [128, 16384], BF16)
        flush_taps()

    def scan_pass(tcs, init_from, update_state):
        for pair in range(2):
            ph = Ph(k)
            ph.dma("gpsimd", WT[:, 0:4096],
                   bass.AP(wz_d[pair], 0, [[512, 128], [65536, 8], [1, 512]]))
            ph.dma("gpsimd", WT[:, 4096:8192],
                   bass.AP(wh_d[pair], 0, [[512, 128], [65536, 8], [1, 512]]))
            ph.wait("tensor", "sG", k.v["sG"])
            for tc in tcs:
                for pt in range(4):
                    spt = 4 * pair + pt
                    psZ = PF[:, 512 * (pt % 2):512 * (pt % 2) + 256]
                    psH = PF[:, 1024 + 512 * (pt % 2):1024 + 512 * (pt % 2) + 256]
                    ph.wait("tensor", "sS", max(0, k.v["sS"] - 1))
                    ph.wait("tensor", "sV", max(0, k.v["sV"] - 4))
                    for kc in range(8):
                        rhs = bass.AP(A1, 2048 * kc + 256 * tc, [[16384, 128], [1, 256]])
                        ph.op("tensor", lambda E, psZ=psZ, kc=kc, pt=pt, r=rhs,
                              st=(kc == 0), sp=(kc == 7), pair=pair: E.matmul(
                                  psZ, WT[:, 512 * kc + 128 * pt:512 * kc + 128 * pt + 128],
                                  r, start=st, stop=sp), inc=False)
                    for kc in range(8):
                        rhs = bass.AP(A1, 2048 * kc + 256 * tc, [[16384, 128], [1, 256]])
                        ph.op("tensor", lambda E, psH=psH, kc=kc, pt=pt, r=rhs,
                              st=(kc == 0), sp=(kc == 7): E.matmul(
                                  psH, WT[:, 4096 + 512 * kc + 128 * pt:
                                          4096 + 512 * kc + 128 * pt + 128],
                                  r, start=st, stop=sp), inc=(kc == 7))
                    ph.wait("scalar", "sT", k.v["sT"])
                    ph.wait("scalar", "sV", k.v["sV"])
                    at = fa(512 * pt, 512 * pt + 256)
                    bt = fa(2048 + 512 * pt, 2048 + 512 * pt + 256)
                    zt = fa(4096 + 256 * (pt % 2), 4096 + 256 * (pt % 2) + 256)
                    ht = fa(4608 + 256 * (pt % 2), 4608 + 256 * (pt % 2) + 256)
                    ph.op("scalar", lambda E, psZ=psZ, at=at, pair=pair, pt=pt:
                          E.activation(at, psZ, AF.Sigmoid, scale=-1.0,
                                       bias=BZN[pair][:, pt:pt + 1]))
                    ph.op("scalar", lambda E, psZ=psZ, zt=zt, pair=pair, pt=pt:
                          E.activation(zt, psZ, AF.Sigmoid,
                                       bias=BZ[pair][:, pt:pt + 1]))
                    sh = ph.op("scalar", lambda E, psH=psH, ht=ht, pair=pair, pt=pt:
                               E.activation(ht, psH, AF.Identity,
                                            bias=BH[pair][:, pt:pt + 1]), inc=True)
                    ph.wait("vector", "sS", sh)
                    vm = ph.op("vector", lambda E, zt=zt, ht=ht, bt=bt:
                               E.tensor_mul(bt, zt, ht), inc=True)
                    sc = fa(5120 + 256 * (pt % 2), 5120 + 256 * (pt % 2) + 256)
                    init = init_from(spt)
                    ph.wait("vector", "sV", vm)
                    vs = ph.op("vector", lambda E, at=at, bt=bt, sc=sc, init=init:
                               E.tensor_tensor_scan(sc, at, bt, init,
                                                    op0=ALU.mult, op1=ALU.add),
                               inc=True)
                    ph.wait("vector", "sV", vs)
                    ph.op("vector", lambda E, sc=sc, spt=spt, tc=tc: E.tensor_copy(
                        A3[:, 2048 * spt + 256 * tc:2048 * spt + 256 * tc + 256], sc))
                    vu = ph.op("vector", lambda E, sc=sc, spt=spt: E.tensor_copy(
                        update_state(spt), sc[:, 255:256]), inc=True)
                    ph.wait("vector", "sV", vu)
            ph.emit()

    if upto >= 8:
        scan_pass(list(range(8)), lambda spt: STATE[:, spt:spt + 1],
                  lambda spt: STATE[:, spt:spt + 1])
        tap("hcomb", A3[:, :], [128, 16384], BF16)
        tap("state", STATE[:, :], [128, 8])
        flush_taps()

    if upto >= 9:
        ph = Ph(k)
        ph.wait("gpsimd", "sV", k.v["sV"])
        ph.dma("gpsimd", agin[:, :], STATE[:, :])
        ph.wait("gpsimd", "sG", k.v["sG"])
        cv = k.bump("cc")
        ph._add("gpsimd", lambda E: E.collective_compute(
            "AllGather", ALU.bypass, replica_groups=[list(range(8))],
            ins=[agin.ap().opt()], outs=[agout.ap().opt()]).then_inc(sems["cc"], 1))
        ph.wait("gpsimd", "cc", cv)
        ph.dma("gpsimd", GATH[:, :], bass.AP(agout, 0, [[8, 128], [1024, 8], [1, 8]]))
        ph.wait("vector", "sG", k.v["sG"])
        ph.op("vector", lambda E: E.tensor_mul(MSKD[:, :], GATH[:, :], SELM[:, :]))
        ph.op("vector", lambda E: E.memset(CARRY[:, :], 0.0))
        ph.op("vector", lambda E: E.engine_nop(), inc=True)
        ph.wait("vector", "sV", k.v["sV"])
        for c in range(8):
            cv_ = ph.op("vector", lambda E, c=c: E.tensor_add(
                CARRY[:, :], CARRY[:, :], MSKD[:, 8 * c:8 * c + 8]), inc=True)
            ph.wait("vector", "sV", cv_)
        ph.emit()
        tap("carry", CARRY[:, :], [128, 8])
        flush_taps()

    if upto >= 10:
        ph = Ph(k)
        ph.wait("vector", "sV", k.v["sV"])
        ph.op("vector", lambda E: E.tensor_copy(STATE2[:, :], CARRY[:, :]), inc=True)
        ph.emit()
        scan_pass(list(range(4)), lambda spt: STATE2[:, spt:spt + 1],
                  lambda spt: STATE2[:, spt:spt + 1])
        tap("hcomb2", A3[:, :], [128, 16384], BF16)
        flush_taps()

    if upto >= 11:
        ph = Ph(k)
        ph.dma("gpsimd", WT[:, :],
               bass.AP(wg_d, 0, [[1024, 128], [131072, 8], [1, 1024]]))
        ph.wait("tensor", "sG", k.v["sG"])
        for nb in range(4):
            mmv = []
            for m in range(8):
                psP = PF[:, 512 * (m % 4):512 * (m % 4) + 512]
                ph.wait("tensor", "sS", max(0, k.v["sS"] - 3))
                for kc in range(8):
                    rhs = bass.AP(A3, 2048 * kc + 512 * nb, [[16384, 128], [1, 512]])
                    ph.op("tensor", lambda E, psP=psP, kc=kc, m=m, r=rhs,
                          st=(kc == 0), sp=(kc == 7): E.matmul(
                              psP, WT[:, 1024 * kc + 128 * m:1024 * kc + 128 * m + 128],
                              r, start=st, stop=sp), inc=(kc == 7))
                ph.wait("scalar", "sT", k.v["sT"])
                ph.wait("scalar", "sV", max(0, k.v["sV"] - 8))
                ph.op("scalar", lambda E, psP=psP, m=m: E.activation(
                    SCB[:, 512 * m:512 * m + 512], psP, AF.Sigmoid,
                    bias=BG[:, m:m + 1]), inc=True)
            ph.wait("vector", "sS", k.v["sS"])
            for m in range(8):
                ph.op("vector", lambda E, m=m, nb=nb: E.tensor_mul(
                    A3[:, 2048 * m + 512 * nb:2048 * m + 512 * nb + 512],
                    SCB[:, 512 * m:512 * m + 512],
                    A3[:, 2048 * m + 512 * nb:2048 * m + 512 * nb + 512]),
                    inc=(m == 7))
            ph.wait("tensor", "sV", k.v["sV"])
        for m in range(8):
            ph.wait("sync", "sV", k.v["sV"])
            ph.dma("sync", bass.AP(s1s, NOWN * 128 * m, [[NOWN, 128], [1, 2048]]),
                   A3[:, 2048 * m:2048 * m + 2048])
        ph.emit()
        tap("s1", A3[:, :], [128, 16384], BF16)
        tapd("s1s", s1s, [8, 128, NOWN], BF16)
        flush_taps()

    if upto >= 11.7:
        psum_ctx.close()
        PF2 = ctx.enter_context(nc.psum_tensor("PF2", [128, 4096], F32))

        ph = Ph(k)
        ph.wait("vector", "sL", k.v["sL"])
        w1sl = {}
        for kc in range(8):
            for g in range(4):
                sl = (kc * 4 + g) % 2
                wst = fa(4096 + 1024 * sl, 4096 + 1024 * (sl + 1))
                if sl in w1sl:
                    ph.wait("sync", "sV", w1sl[sl])
                lw = ph.dma("sync", wst, bass.AP(
                    w1t_d, 4096 * 128 * kc + 1024 * g, [[4096, 128], [1, 1024]]))
                idx = kc * 32 + g * 8
                if idx < 128:
                    dst = A4[:, 128 * idx:128 * idx + 1024]
                else:
                    dst = A3[:, 128 * (idx - 128):128 * (idx - 128) + 1024]
                t1b = SCB[:, 0:1024]; t2b = SCB[:, 1024:2048]
                ph.wait("vector", "sL", lw)
                ph.op("vector", lambda E, wst=wst, t1b=t1b: E.tensor_scalar(
                    t1b, wst, THR[:, 0:1], None, op0=ALU.is_ge))
                qv = ph.op("vector", lambda E, wst=wst, t2b=t2b: E.tensor_scalar(
                    t2b, wst, THR[:, 1:2], None, op0=ALU.is_le), inc=True)
                ph.wait("vector", "sV", qv)
                w1sl[sl] = ph.op(
                    "vector", lambda E, t1b=t1b, t2b=t2b, dst=dst: E.tensor_sub(
                        dst, t1b, t2b), inc=True)
        ph.emit()

        def w1chunk(kc, m):
            idx = kc * 32 + m
            if idx < 128:
                return A4[:, 128 * idx:128 * idx + 128]
            return A3[:, 128 * (idx - 128):128 * (idx - 128) + 128]

        tclist = [3] if upto < 11.9 else list(range(4))
        full_epi = upto >= 11.75
        for tc in tclist:
            ph = Ph(k)
            w2_sem = {}
            for half in range(2):
                ms = range(16 * half, 16 * half + 16)
                for m in ms:
                    psH = PF2[:, 512 * (m % 2):512 * (m % 2) + 512]
                    ph.wait("tensor", "sS", max(0, k.v["sS"] - 1))
                    for kc in range(8):
                        rhs = bass.AP(A1, 2048 * kc + 512 * tc, [[16384, 128], [1, 512]])
                        ph.op("tensor", lambda E, psH=psH, kc=kc, m=m, r=rhs,
                              st=(kc == 0), sp=(kc == 7): E.matmul(
                                  psH, w1chunk(kc, m), r, start=st, stop=sp),
                              inc=(kc == 7))
                    ph.wait("scalar", "sT", k.v["sT"])
                    ph.op("scalar", lambda E, psH=psH, m=m: E.activation(
                        A2[:, 512 * (m % 16):512 * (m % 16) + 512], psH, AF.Relu,
                        bias=B1[:, m:m + 1]), inc=True)
                ph.wait("tensor", "sS", k.v["sS"])
                for mi, m in enumerate(ms):
                    w2c = A6[:, 2048 * (m % 2):2048 * (m % 2) + 1024]
                    if (m % 2) in w2_sem:
                        ph.wait("sync", "sT", w2_sem[m % 2])
                    lw = ph.dma("sync", w2c, wq2s[128 * m:128 * m + 128, :])
                    ph.wait("tensor", "sL", lw)
                    for mo in range(8):
                        psO = PF2[:, 512 * mo:512 * mo + 512]
                        st = (mi == 0)
                        vmm = ph.op("tensor", lambda E, psO=psO, w2c=w2c, mo=mo, m=m,
                                    st=st: E.matmul(
                                        psO, w2c[:, 128 * mo:128 * mo + 128],
                                        A2[:, 512 * (m % 16):512 * (m % 16) + 512],
                                        start=st, stop=False), inc=True)
                        ph.wait("tensor", "sT", k.v["sT"])
                        if mo == 7:
                            w2_sem[m % 2] = vmm
                if half == 0:
                    ph.wait("vector", "sT", k.v["sT"])
                    for mo in range(8):
                        ph.op("vector", lambda E, mo=mo: E.tensor_copy(
                            fa(512 * mo, 512 * mo + 512),
                            PF2[:, 512 * mo:512 * mo + 512]), inc=(mo == 7))
                    ph.wait("tensor", "sV", k.v["sV"])
            if upto < 11.9 and not full_epi:
                ph.wait("vector", "sT", k.v["sT"])
                for mo in range(8):
                    psO = PF2[:, 512 * mo:512 * mo + 512]
                    z1 = fa(512 * mo, 512 * mo + 512)
                    ph.op("vector", lambda E, psO=psO, z1=z1, mo=mo:
                          E.scalar_tensor_tensor(z1, psO, B2[:, mo:mo + 1], z1,
                                                 op0=ALU.add, op1=ALU.add),
                          inc=(mo == 7))
                ph.emit()
                tap("psO", FA[:, 0:4096], [128, 4096])
                flush_taps()
                break
            ph.wait("vector", "sT", k.v["sT"])
            sgp = []
            sgm_sem = {}
            silu_sem = {}
            for mo in range(8):
                psO = PF2[:, 512 * mo:512 * mo + 512]
                z1 = fa(512 * mo, 512 * mo + 512)
                z1t = fa(4096 + 512 * (mo % 2), 4096 + 512 * (mo % 2) + 512)
                if (mo % 2) in silu_sem:
                    ph.wait("vector", "sS", silu_sem[mo % 2])
                vv = ph.op("vector", lambda E, psO=psO, z1=z1, z1t=z1t:
                           E.tensor_add(z1t, psO, z1), inc=True)
                ph.wait("scalar", "sV", vv)
                if (mo % 4) in sgm_sem:
                    ph.wait("scalar", "sV", sgm_sem[mo % 4])
                g1 = SCB[:, 512 * (mo % 4):512 * (mo % 4) + 512]
                sv = ph.op("scalar", lambda E, z1t=z1t, g1=g1, mo=mo: E.activation(
                    g1, z1t, AF.Silu, bias=B2[:, mo:mo + 1]), inc=True)
                silu_sem[mo % 2] = sv
                s1c = SCB[:, 2048 + 512 * (mo % 4):2048 + 512 * (mo % 4) + 512]
                if (mo % 4) in sgm_sem:
                    ph.wait("sync", "sV", sgm_sem[mo % 4])
                lw = ph.dma("sync", s1c, bass.AP(
                    s1s, NOWN * 128 * mo + 512 * tc, [[NOWN, 128], [1, 512]]))
                ph.wait("vector", "sS", sv)
                ph.wait("vector", "sL", lw)
                sgm_sem[mo % 4] = ph.op(
                    "vector", lambda E, g1=g1, s1c=s1c, z1=z1: E.tensor_mul(
                        z1, g1, s1c), inc=True)
                sgp.append(z1)
            ph.wait("sync", "sS", k.v["sS"])
            for tt in range(4):
                pb2 = PF2[:, 1024 * (tt % 2):1024 * (tt % 2) + 1024]
                ph.wait("tensor", "sV", k.v["sV"])
                for m in range(8):
                    ph.op("tensor", lambda E, pb2=pb2, m=m, sg=sgp[m], tt=tt:
                          E.transpose(pb2[:, 128 * m:128 * m + 128],
                                      sg[:, 128 * tt:128 * tt + 128], IDENF[:, :]),
                          inc=(m == 7))
                row = 512 * tc + 128 * tt
                xt = fa(4096, 5120)
                xo = fa(5120, 6144)
                lw = ph.dma("sync", xt, x1s[row:row + 128, :])
                ph.wait("vector", "sT", k.v["sT"])
                ph.wait("vector", "sL", lw)
                cv = ph.op("vector", lambda E, pb2=pb2, xt=xt, xo=xo: E.tensor_add(
                    xo, pb2, xt), inc=True)
                ph.wait("sync", "sV", cv)
                ph.dma("sync", y_d[row:row + 128, :], xo)
                ph.wait("vector", "sL", k.v["sL"])
            ph.emit()

    if upto >= 11.75:
        tap("h1last", A2[:, :], [128, 8192], BF16)
        tap("sg1last", FA[:, 0:4096], [128, 4096])
        tap("g1last", SCB[:, 0:2048], [128, 2048], BF16)
        tap("s1clast", SCB[:, 2048:4096], [128, 2048], BF16)
        flush_taps()

    ph = Ph(k)
    ph.wait("sync", "sL")
    ph.wait("sync", "sG")
    ph.emit()
    flush_taps()

    return nc, ctx


def _make_core_inputs(c, ins):
    b, s0 = c // 4, 2048 * (c % 4)
    x = np.asarray(ins["L0_attn_x"])[b]
    if s0 == 0:
        xs = np.concatenate([np.zeros((1024, D), np.float32), x[0:2048]], axis=0)
    else:
        xs = x[s0 - 1024:s0 + 2048]
    first = s0 == 0

    inv = (1.0 / 10000.0 ** (np.arange(0, 64, 2, dtype=np.float64) / 64.0))
    pos_k = np.arange(s0 - 1024, s0 + 2048, dtype=np.float64).clip(0)
    pos_q = np.arange(s0, s0 + 2048, dtype=np.float64)

    def tabs(pos, scale):
        ang = pos[:, None] * inv[None, :]
        cos = np.cos(ang); sin = np.sin(ang)
        cosT = np.concatenate([cos, cos], axis=1) * scale
        sinT = np.concatenate([-sin, sin], axis=1) * scale
        return (cosT.astype(ml_dtypes.bfloat16), sinT.astype(ml_dtypes.bfloat16))

    cos_k, sinm_k = tabs(pos_k, 1.0)
    cos_q, sinm_q = tabs(pos_q, 0.125)

    causal = np.triu(np.ones((128, 128), np.float32))
    flags = np.zeros((128, 2), np.float32)
    flags[:, 0] = -1.0e9 + CSHIFT if first else CSHIFT
    flags[:, 1] = CSHIFT
    selm = np.zeros((128, 64), np.float32)
    if not first:
        selm[:, 8 * (c - 1):8 * (c - 1) + 8] = 1.0
    sel31 = np.zeros((128, 64), np.float32)
    sel31[63, :] = 1.0

    r8 = lambda v: np.ascontiguousarray(np.asarray(v, np.float32).reshape(-1, 128).T)

    m = {
        "x_in": np.ascontiguousarray(xs, dtype=np.float32),
        "wq": np.asarray(ins["L0_attn_Wq"], np.float32),
        "wkv": np.asarray(ins["L0_attn_Wkv"], np.float32),
        "wo_a": np.asarray(ins["L0_attn_Wo0"] if first else ins["L0_attn_Wo"], np.float32),
        "wo_b": np.asarray(ins["L0_attn_Wo"], np.float32),
        "bo_a": r8(ins["L0_attn_bo0"] if first else ins["L0_attn_bo"]),
        "bo_b": r8(ins["L0_attn_bo"]),
        "w0t": np.ascontiguousarray(np.asarray(ins["L0_ffn_weight"], np.float32).T),
        "wzf": np.asarray(ins["L1_attn_W_zf"], np.float32),
        "whf": np.asarray(ins["L1_attn_W_hf"], np.float32),
        "wzs": np.asarray(ins["L1_attn_W_zs"], np.float32),
        "whs": np.asarray(ins["L1_attn_W_hs"], np.float32),
        "bzf": r8(ins["L1_attn_b_zf"]), "bhf": r8(ins["L1_attn_b_hf"]),
        "bzs": r8(ins["L1_attn_b_zs"]), "bhs": r8(ins["L1_attn_b_hs"]),
        "wg": np.asarray(ins["L1_attn_W_g"], np.float32),
        "bg": r8(ins["L1_attn_b_g"]),
        "w1t": np.ascontiguousarray(np.asarray(ins["L1_ffn_W1"], np.float32).T),
        "b1": r8(ins["L1_ffn_b1"]),
        "w2t": np.ascontiguousarray(np.asarray(ins["L1_ffn_W2"], np.float32).T),
        "b2": r8(ins["L1_ffn_b2"]),
        "cos_k": cos_k, "sinm_k": sinm_k, "cos_q": cos_q, "sinm_q": sinm_q,
        "iden": np.eye(128, dtype=np.float32),
        "causal": causal,
        "flags": flags,
        "selmask": selm,
        "sel31": sel31,
    }
    return m


def kernel(**inputs):
    nc, ctx = build_program()
    try:
        in_maps = [_make_core_inputs(c, inputs) for c in range(8)]
        res = run_bass_kernel_spmd(nc, in_maps, core_ids=list(range(8)))
    finally:
        ctx.close()
    out = np.empty((2, 8192, D), np.float32)
    for c in range(8):
        b, s0 = c // 4, 2048 * (c % 4)
        out[b, s0:s0 + 2048] = res.results[c]["y"]
    return out
